# version 05
# baseline (speedup 1.0000x reference)
import numpy as np
import ml_dtypes
from contextlib import ExitStack

import concourse.bass as bass
import concourse.mybir as mybir
from concourse.bass_utils import run_bass_kernel_spmd

F32 = mybir.dt.float32
F16 = mybir.dt.float16
BF16 = mybir.dt.bfloat16
I32 = mybir.dt.int32
OP = mybir.AluOpType
AF = mybir.ActivationFunctionType

D = 2048
E = 4096
NST = 16
DTR = 128
HID = 8192
L = 8192
R = 4
TB = 512
TBH = TB + 3
TP = TB + 4
EQ = 1024
DPG, EPG, HPG, QPG = 16, 32, 64, 8
TS = 512
NSS = 4
TE = 256
G1ROWS = E + DTR + 2 * NST
QT_ROWS = 128 * (16 * 8 + 64 * 2)

ENGINES = ("sync", "gpsimd", "vector", "scalar", "tensor")
ESEM = {"sync": "sy", "gpsimd": "gp", "vector": "v", "scalar": "sc", "tensor": "pe"}


class Plan:
    def __init__(self):
        self.steps = {e: [] for e in ENGINES}
        self.cnt = {}
        self.wr = {}
        self.rd = {}

    def step(self, eng, fn, reads=(), writes=(), sem=None, inc=1, writes_acc=()):
        sem = sem if sem is not None else ESEM[eng]
        self.cnt.setdefault(sem, 0)
        waits = {}

        def add(s, v):
            if v > 0 and waits.get(s, 0) < v:
                waits[s] = v

        for k in reads:
            w = self.wr.get(k)
            if w:
                add(*w)
        for k in writes:
            w = self.wr.get(k)
            if w:
                add(*w)
            for rv in self.rd.get(k, ()):
                add(*rv)
        v1 = self.cnt[sem] + inc
        self.cnt[sem] = v1
        marker = (sem, v1) if inc > 0 else None
        for k in tuple(writes) + tuple(writes_acc):
            if marker:
                self.wr[k] = marker
            if k in writes:
                self.rd[k] = []
        if marker:
            for k in reads:
                self.rd.setdefault(k, []).append(marker)
        self.steps[eng].append((sorted(waits.items()), fn, sem, inc))

    def replay(self, eng_name, eng, sems):
        seen = {}
        for waits, fn, sem, inc in self.steps[eng_name]:
            for s, v in waits:
                if seen.get(s, 0) < v:
                    eng.wait_ge(sems[s], v)
                    seen[s] = v
            fn(eng, sems[sem] if inc > 0 else None)


def build_program():
    nc = bass.Bass("TRN2", target_bir_lowering=False, detect_race_conditions=False)

    IO = {}
    def din(name, shape, dt=F32):
        IO[name] = nc.dram_tensor(name, shape, dt, kind="ExternalInput").ap()
    din("xt", [R * D, TBH])
    din("w_in4", [EPG * 128, DPG * 128 + DTR + 2 * NST])
    din("w_sk4", [EPG * 128, DPG * 128])
    din("w_out4", [DPG * 128, EPG * 128])
    din("wd2t_own", [DTR, EQ])
    din("alog_own", [128, QPG * NST])
    din("convw_t", [128, EPG * 4])
    din("cb_t", [128, EPG])
    din("bd2_own", [128, QPG])
    din("wdv_t", [128, EPG])
    din("nw_t", [128, DPG])
    din("wfc_sh", [HID // 8, D])
    din("wpr_sh", [D // 8, HID])
    din("ctab", [1, 80], I32)
    IO["out"] = nc.dram_tensor("out_t", [D, R * TB], F32, kind="ExternalOutput").ap()

    IO["g1in"] = [nc.dram_tensor(f"g1in{r}", [G1ROWS, TB], F16).ap() for r in range(R)]
    IO["g1out"] = [nc.dram_tensor(f"g1out{r}", [4 * G1ROWS, TB], F16).ap() for r in range(R)]
    IO["g2in"] = [nc.dram_tensor(f"g2in{r}", [EQ, R * TB], F16).ap() for r in range(R)]
    IO["g2out"] = [nc.dram_tensor(f"g2out{r}", [E, R * TB], F16).ap() for r in range(R)]
    IO["skip_s"] = [nc.dram_tensor(f"skips{r}", [E, TB], F16).ap() for r in range(R)]
    IO["wq_in"] = nc.dram_tensor("wq_in", [QT_ROWS, 128], F16).ap()
    IO["wq_out"] = nc.dram_tensor("wq_out", [8 * QT_ROWS, 128], F16).ap()
    import os
    IO["_debug"] = bool(int(os.environ.get("MK_DEBUG", "0")))
    if IO["_debug"]:
        for nm, shp in (("dbg_g1in0", [G1ROWS, TB]), ("dbg_g1out0", [4 * G1ROWS, TB]),
                        ("dbg_skips0", [E, TB]), ("dbg_g2in0", [EQ, R * TB]),
                        ("dbg_g2out0", [E, R * TB]), ("dbg_wqin", [QT_ROWS, 128]),
                        ("dbg_y2", [128, EPG * TB]), ("dbg_omx", [128, 16384]),
                        ("dbg_x1b", [128, DPG * TB]), ("dbg_xn2", [128, EPG * TB]),
                        ("dbg_h2", [128, 16384]), ("dbg_opre", [128, DPG * TE])):
            IO[nm] = nc.dram_tensor(nm, shp, F16, kind="ExternalOutput").ap()

    plan = Plan()
    es = ExitStack()
    T = {}

    def sb(name, shape, dt):
        T[name] = es.enter_context(nc.sbuf_tensor(name, shape, dt))

    def psum(name, shape):
        T[name] = es.enter_context(nc.psum_tensor(name, shape, F32))

    with es:
        sb("ones1_16", [1, 128], F16)
        sb("onesneg16", [1, 128], F16)
        sb("onescol", [128, 1], F16)
        sb("identf", [128, 128], F32)
        sb("idio", [128, 128], I32)
        sb("a_t", [128, QPG * NST], F32)
        sb("convw", [128, EPG * 4], F32)
        sb("cbv", [128, EPG], F32)
        sb("bd2v", [128, QPG], F32)
        sb("wdv", [128, EPG], F32)
        sb("nwv", [128, DPG], F32)
        sb("wd2s", [128, EQ], F16)
        sb("hstate", [128, QPG * NST], F32)
        sb("rowbuf", [1, 5 * TBH], F32)
        sb("row16", [1, 2 * TP], F16)
        sb("ctabs", [1, 80], I32)
        sb("cpool", [128, 4], F32)
        nc.const_aps.register(F32, 1e-5, T["cpool"][:, 0:1])
        nc.const_aps.register(F32, 1.0, T["cpool"][:, 1:2])
        nc.const_aps.register(F32, -0.5, T["cpool"][:, 2:3])
        nc.const_aps.register(F32, -1.0, T["cpool"][:, 3:4])
        sb("xn", [128, DPG * TP], F16)
        sb("abb", [128, 2 * TP], F16)
        sb("wsl0", [128, 16 * 128 + DTR + 2 * NST], F16)
        sb("wsl1", [128, 16 * 128 + DTR + 2 * NST], F16)
        sb("u0p0", [128, TBH], F16)
        sb("u0p1", [128, TBH], F16)
        sb("cvt0", [128, TB], F16)
        sb("cvt1", [128, TB], F16)
        sb("upg0", [128, TB], F16)
        sb("upg1", [128, TB], F16)
        sb("d1c", [128, TB], F16)
        sb("bct", [32, TB], F16)
        sb("skp0", [128, TB], F16)
        sb("skp1", [128, TB], F16)
        sb("od", [128, QPG * TS], F16)
        sb("oz", [128, QPG * TS], F16)
        sb("oua", [128, QPG * TS], F16)
        sb("od1", [128, TS], F16)
        sb("ot1", [128, TS], F16)
        sb("obr0", [1, 2 * TS], F16)
        sb("obr1", [1, 2 * TS], F16)
        sb("obb0", [128, 2 * TS], F16)
        sb("obb1", [128, 2 * TS], F16)
        sb("oa0", [128, TS], F16)
        sb("oa1", [128, TS], F16)
        sb("ob0", [128, TS], F16)
        sb("ob1", [128, TS], F16)
        sb("oh0", [128, TS], F16)
        sb("oh1", [128, TS], F16)
        sb("otm0", [128, TS], F16)
        sb("otm1", [128, TS], F16)
        sb("oy", [128, QPG * TS], F16)
        sb("y2", [128, EPG * TB], F16)

        sb("sk20", [128, TB], F16)
        sb("sk21", [128, TB], F16)
        sb("big1", [128, 16384], F16)
        sb("x1b", [128, DPG * TB], F16)
        sb("xr32", [128, TB], F32)
        sb("opre", [128, DPG * TE], F16)
        sb("rl0", [128, TE], F16)
        sb("rl1", [128, TE], F16)
        sb("of32", [128, TE], F32)
        sb("cdt", [128, TB], F16)
        sb("qsc", [128, 64], F32)
        sb("qsbf", [128, 32], BF16)
        psum("p1", [128, TP])
        psum("p2", [128, TP])
        psum("pown", [128, TS])
        psum("pd", [128, TB])
        psum("pe1", [128, TE])
        psum("p5", [32, TB])

        rq0 = es.enter_context(nc.gpsimd.register("rq0"))
        rq1 = es.enter_context(nc.gpsimd.register("rq1"))
        rtmp = es.enter_context(nc.gpsimd.register("rtmp"))

        build_plan(nc, plan, T, IO, (rq0, rq1, rtmp))

        sems = {}
        for name in plan.cnt:
            sems[name] = es.enter_context(nc.semaphore(name))
        blk = es.enter_context(nc.Block())
        blk.sync(lambda e: plan.replay("sync", e, sems))
        blk.gpsimd(lambda e: plan.replay("gpsimd", e, sems))
        blk.vector(lambda e: plan.replay("vector", e, sems))
        blk.scalar(lambda e: plan.replay("scalar", e, sems))
        blk.tensor(lambda e: plan.replay("tensor", e, sems))
    return nc


def build_plan(nc, plan, T, IO, regs):
    rq0, rq1, rtmp = regs
    GRP4 = [[0, 1, 2, 3], [4, 5, 6, 7]]
    GRP8 = [[0, 1, 2, 3, 4, 5, 6, 7]]

    def dma(eng, pairs, reads, writes, sem, writes_acc=()):
        def fn(e, s, pairs=tuple(pairs)):
            for dst, src in pairs:
                e.dma_start(out=dst(), in_=src()).then_inc(s, 16)
        plan.step(eng, fn, reads=reads, writes=writes, sem=sem,
                  inc=16 * len(pairs), writes_acc=writes_acc)

    def vstep(fn, reads, writes):
        plan.step("vector", fn, reads=reads, writes=writes)

    def sstep(fn, reads, writes):
        plan.step("scalar", fn, reads=reads, writes=writes)

    def gstep(fn, reads, writes):
        plan.step("gpsimd", fn, reads=reads, writes=writes)

    def pestep(fn, reads, writes):
        plan.step("tensor", fn, reads=reads, writes=writes)

    def ccstep(kind_groups, ins_fn, outs_fn, reads, writes, sem):
        def fn(e, s):
            e.collective_compute("AllGather", OP.bypass, replica_groups=kind_groups,
                                 ins=[ins_fn()], outs=[outs_fn()]).then_inc(s)
        plan.step("gpsimd", fn, reads=reads, writes=writes, sem=sem, inc=1)

    def init_const(e, s):
        e.memset(T["cpool"][:, 0:1], 1e-5)
        e.memset(T["cpool"][:, 1:2], 1.0)
        e.memset(T["cpool"][:, 2:3], -0.5)
        e.memset(T["cpool"][:, 3:4], -1.0)
        e.memset(T["ones1_16"][:], 1.0)
        e.memset(T["onesneg16"][:], -1.0)
        e.memset(T["onescol"][:], 1.0)
        e.memset(T["hstate"][:], 0.0)
        e.memset(T["xn"][:], 0.0)
        e.memset(T["row16"][:], 0.0)
        e.memset(T["abb"][:], 0.0).then_inc(s, 1)
    plan.step("vector", init_const, writes=("ones1_16", "onesneg16", "onescol", "hstate", "cpool", "xn", "row16", "abb"))

    def init_iota(e, s):
        e.iota(T["idio"][:], [[1, 128]], channel_multiplier=-1)
        e.drain()
        e.tensor_scalar(T["identf"][:], T["idio"][:], 0, None, OP.is_equal).then_inc(s, 1)
    dma("sync", [(lambda: T["ctabs"][:], lambda: IO["ctab"][:, :])],
        reads=(), writes=("ctabs",), sem="d_small")
    plan.step("gpsimd", init_iota, reads=("ctabs",), writes=("identf", "idio"))

    dma("sync", [
        (lambda: T["convw"][:], lambda: IO["convw_t"][:, :]),
        (lambda: T["cbv"][:], lambda: IO["cb_t"][:, :]),
        (lambda: T["bd2v"][:], lambda: IO["bd2_own"][:, :]),
        (lambda: T["wdv"][:], lambda: IO["wdv_t"][:, :]),
        (lambda: T["nwv"][:], lambda: IO["nw_t"][:, :]),
        (lambda: T["a_t"][:], lambda: IO["alog_own"][:, :]),
    ], reads=(), writes=("convw", "cbv", "bd2v", "wdv", "nwv", "a_t"), sem="d_small")

    def a_transform(e, s):
        e.activation(T["a_t"][:], T["a_t"][:], AF.Exp)
        e.drain()
        e.activation(T["a_t"][:], T["a_t"][:], AF.Copy, scale=-1.0).then_inc(s, 1)
    sstep(a_transform, reads=("a_t", "cpool"), writes=("a_t",))

    dma("gpsimd", [
        (lambda: T["wd2s"][:], lambda: IO["wd2t_own"][:, :]),
    ], reads=(), writes=("wd2s",), sem="d_smallw")

    def quant_tile(src_fn, dst_row_of_j):
        dma("sync", [(lambda: T["big1"][:].bitcast(F32)[:, 0:2048], src_fn)],
            reads=("big1",), writes=("big1",), sem="d_q")

        def qmath(e, s):
            qb = T["xn"][:].bitcast(BF16)[:, 0:2048]
            e.tensor_copy(qb, T["big1"][:].bitcast(F32)[:, 0:2048])
            e.drain()
            e.tensor_reduce(T["qsc"][:, 0:32].rearrange("p (g o) -> p g o", o=1),
                            qb.rearrange("p (g w) -> p g w", w=64),
                            mybir.AxisListType.X, OP.add, apply_absolute_value=True)
            e.drain()
            e.tensor_scalar(T["qsc"][:, 0:32], T["qsc"][:, 0:32], 1.0 / 64, None, OP.mult)
            e.drain()
            e.tensor_copy(T["qsbf"][:], T["qsc"][:, 0:32])
            e.drain()
            e.tensor_scalar(T["qsbf"][:], T["qsbf"][:], 1e-8, None, OP.max)
            e.drain()
            e.reciprocal(T["qsc"][:, 32:64], T["qsbf"][:])
            e.drain()
            rb = T["xn"][:].bitcast(BF16)[:, 2048:4096]
            e.tensor_tensor(rb.rearrange("p (g w) -> p g w", w=64),
                            qb.rearrange("p (g w) -> p g w", w=64),
                            T["qsc"][:, 32:64].rearrange("p (g o) -> p g o", o=1).broadcast_to([128, 32, 64]),
                            OP.mult)
            e.drain()
            t1 = T["x1b"][:].bitcast(BF16)[:, 0:2048]
            e.tensor_scalar(t1, rb, 0.5, None, OP.is_gt)
            e.tensor_scalar(T["opre"][:].bitcast(BF16)[:, 0:2048], rb, -0.5, None, OP.is_lt)
            e.drain()
            e.tensor_tensor(t1, t1, T["opre"][:].bitcast(BF16)[:, 0:2048], OP.subtract)
            e.drain()
            w1 = T["y2"][:].bitcast(F32)[:, 0:2048]
            e.tensor_tensor(w1.rearrange("p (g w) -> p g w", w=64),
                            t1.rearrange("p (g w) -> p g w", w=64),
                            T["qsbf"][:].rearrange("p (g o) -> p g o", o=1).broadcast_to([128, 32, 64]),
                            OP.mult).then_inc(s, 1)
        plan.step("vector", qmath, reads=("big1",),
                  writes=("xn", "qsc", "qsbf", "x1b", "opre", "y2"))

        for j4 in range(4):
            def tp(e, s, j4=j4):
                for jj in range(4):
                    j = 4 * j4 + jj
                    ins = e.transpose(T["pd"][:, 128 * jj:128 * (jj + 1)],
                                      T["y2"][:].bitcast(F32)[:, 128 * j:128 * (j + 1)],
                                      T["identf"][:])
                ins.then_inc(s, 1)
            pestep(tp, reads=("y2", "identf"), writes=("pd",))

            def ev(e, s):
                e.activation(T["opre"][:, 0:TB], T["pd"][:, 0:TB], AF.Copy).then_inc(s, 1)
            sstep(ev, reads=("pd",), writes=("opre",))

            rows = [dst_row_of_j(4 * j4 + jj) for jj in range(4)]
            dma("sync", [(lambda row=rows[jj]: IO["wq_in"][row:row + 128, :],
                          lambda jj=jj: T["opre"][:, 128 * jj:128 * (jj + 1)])
                         for jj in range(4)],
                reads=("opre",), writes=(), writes_acc=("wq_in",), sem="d_qo")

    for t in range(8):
        quant_tile(lambda t=t: IO["wfc_sh"][128 * t:128 * (t + 1), :],
                   lambda j, t=t: 128 * (t * 16 + j))
    for dt_ in range(2):
        for cch in range(4):
            quant_tile(lambda dt_=dt_, cch=cch: IO["wpr_sh"][128 * dt_:128 * (dt_ + 1), 2048 * cch:2048 * (cch + 1)],
                       lambda j, dt_=dt_, cch=cch: 128 * (16 * 8 + dt_ * 64 + cch * 16 + j))

    ccstep(GRP8, lambda: IO["wq_in"], lambda: IO["wq_out"],
           reads=("wq_in",), writes=("wq_out",), sem="cc_q")

    def ln_stats_apply(r, srcbuf, ncols, dstkind, sub=0, halo=False):
        TT = TBH

        sqbuf = "big1" if srcbuf != "big1" else "y2"

        def squares(e, s):
            e.activation(T[sqbuf][:, 0:DPG * ncols],
                         T[srcbuf][:, 0:DPG * ncols], AF.Square).then_inc(s, 1)
        sstep(squares, reads=(srcbuf,), writes=(sqbuf,))

        ranges = [(3, TBH, 3), (0, 4, 0)] if halo else [(0, ncols, 3)]
        for (bufk, dstbase) in ((srcbuf, 0), (sqbuf, TT)):
            for (c0, c1, doff) in ranges:
                ncc = c1 - c0

                def statmm(e, s, bufk=bufk, c0=c0, ncc=ncc):
                    for kt in range(DPG):
                        ins = e.matmul(T["p5"][0:1, 0:ncc], T["onescol"][:, 0:1],
                                       T[bufk][:, ncols * kt + c0:ncols * kt + c0 + ncc],
                                       start=(kt == 0), stop=(kt == DPG - 1))
                    ins.then_inc(s, 1)
                pestep(statmm, reads=(bufk, "onescol"), writes=("p5",))

                nev = min(ncc, 3) if (halo and c0 == 0) else ncc

                def ev(e, s, nev=nev, dst=dstbase + doff):
                    e.activation(T["rowbuf"][0:1, dst:dst + nev],
                                 T["p5"][0:1, 0:nev], AF.Copy).then_inc(s, 1)
                sstep(ev, reads=("p5",), writes=("rowbuf",))

        nstat = TBH if halo else ncols
        soff = 0 if halo else 3

        def rowm_v(e, s):
            rb = T["rowbuf"]
            e.tensor_scalar(rb[0:1, 2 * TT:2 * TT + nstat], rb[0:1, soff:soff + nstat], 1.0 / D, None, OP.mult)
            e.tensor_scalar(rb[0:1, 3 * TT:3 * TT + nstat], rb[0:1, TT + soff:TT + soff + nstat], 1.0 / D, None, OP.mult)
            e.drain()
            e.tensor_tensor(rb[0:1, TT:TT + nstat], rb[0:1, 2 * TT:2 * TT + nstat],
                            rb[0:1, 2 * TT:2 * TT + nstat], OP.mult)
            e.drain()
            e.tensor_tensor(rb[0:1, 3 * TT:3 * TT + nstat], rb[0:1, 3 * TT:3 * TT + nstat],
                            rb[0:1, TT:TT + nstat], OP.subtract).then_inc(s, 1)
        vstep(rowm_v, reads=("rowbuf",), writes=("rowbuf",))

        def rowm_s(e, s):
            rb = T["rowbuf"]
            e.activation(rb[0:1, 4 * TT:4 * TT + nstat], rb[0:1, 3 * TT:3 * TT + nstat], AF.Ln, bias=1e-5)
            e.drain()
            e.activation(rb[0:1, 4 * TT:4 * TT + nstat], rb[0:1, 4 * TT:4 * TT + nstat], AF.Exp, scale=-0.5).then_inc(s, 1)
        sstep(rowm_s, reads=("rowbuf", "cpool"), writes=("rowbuf",))

        if dstkind == "xn":
            def rowm_v2(e, s):
                rb = T["rowbuf"]
                e.tensor_scalar(rb[0:1, 0:nstat], rb[0:1, 3 * TT:3 * TT + nstat], 1e-5, None, OP.add)
                e.drain()
                e.reciprocal(rb[0:1, TT:TT + nstat], rb[0:1, 0:nstat])
                e.drain()
                e.tensor_tensor(rb[0:1, 0:nstat], rb[0:1, 3 * TT:3 * TT + nstat],
                                rb[0:1, TT:TT + nstat], OP.mult).then_inc(s, 1)
            vstep(rowm_v2, reads=("rowbuf",), writes=("rowbuf",))

            def rowm_s2(e, s):
                rb = T["rowbuf"]
                e.activation(rb[0:1, 0:nstat], rb[0:1, 0:nstat], AF.Ln, bias=1e-5)
                e.drain()
                e.activation(rb[0:1, 0:nstat], rb[0:1, 0:nstat], AF.Exp, scale=-0.5).then_inc(s, 1)
            sstep(rowm_s2, reads=("rowbuf", "cpool"), writes=("rowbuf",))

            def rowm_v3(e, s):
                rb = T["rowbuf"]
                e.tensor_tensor(rb[0:1, 4 * TT:4 * TT + nstat], rb[0:1, 4 * TT:4 * TT + nstat],
                                rb[0:1, 0:nstat], OP.mult)
                e.drain()
                e.tensor_scalar(rb[0:1, 4 * TT:4 * TT + nstat], rb[0:1, 4 * TT:4 * TT + nstat],
                                60000.0, None, OP.min)
                e.drain()
                e.tensor_tensor(rb[0:1, 2 * TT:2 * TT + nstat], rb[0:1, 2 * TT:2 * TT + nstat],
                                rb[0:1, 4 * TT:4 * TT + nstat], OP.mult)
                e.drain()
                e.tensor_copy(T["row16"][0:1, 0:nstat], rb[0:1, 4 * TT:4 * TT + nstat])
                e.tensor_copy(T["row16"][0:1, TP:TP + nstat], rb[0:1, 2 * TT:2 * TT + nstat]).then_inc(s, 1)
            vstep(rowm_v3, reads=("rowbuf",), writes=("row16",))
        else:
            def rowm_v3b(e, s):
                rb = T["rowbuf"]
                e.tensor_tensor(rb[0:1, 2 * TT:2 * TT + nstat], rb[0:1, 2 * TT:2 * TT + nstat],
                                rb[0:1, 4 * TT:4 * TT + nstat], OP.mult)
                e.drain()
                e.tensor_copy(T["row16"][0:1, 0:nstat], rb[0:1, 4 * TT:4 * TT + nstat])
                e.tensor_copy(T["row16"][0:1, TP:TP + nstat], rb[0:1, 2 * TT:2 * TT + nstat]).then_inc(s, 1)
            vstep(rowm_v3b, reads=("rowbuf",), writes=("row16",))

        def bcast(e, s):
            if nstat % 2:
                e.matmul(T["p1"][:, 0:TB], T["ones1_16"][0:1, :], T["row16"][0:1, 0:TB],
                         start=True, stop=True)
                e.matmul(T["p1"][:, TB:TP], T["ones1_16"][0:1, :], T["row16"][0:1, TB:TP],
                         start=True, stop=True)
                e.matmul(T["p2"][:, 0:TB], T["onesneg16"][0:1, :], T["row16"][0:1, TP:TP + TB],
                         start=True, stop=True)
                e.matmul(T["p2"][:, TB:TP], T["onesneg16"][0:1, :], T["row16"][0:1, TP + TB:2 * TP],
                         start=True, stop=True).then_inc(s, 1)
            else:
                e.matmul(T["p1"][:, 0:nstat], T["ones1_16"][0:1, :], T["row16"][0:1, 0:nstat],
                         start=True, stop=True)
                e.matmul(T["p2"][:, 0:nstat], T["onesneg16"][0:1, :], T["row16"][0:1, TP:TP + nstat],
                         start=True, stop=True).then_inc(s, 1)
        pestep(bcast, reads=("row16", "ones1_16", "onesneg16"), writes=("p1", "p2"))

        def evb(e, s):
            e.activation(T["abb"][:, 0:nstat], T["p1"][:, 0:nstat], AF.Copy)
            e.activation(T["abb"][:, TP:TP + nstat], T["p2"][:, 0:nstat], AF.Copy).then_inc(s, 1)
        sstep(evb, reads=("p1", "p2"), writes=("abb",))

        for pg in range(DPG):
            if dstkind == "xn":
                def app(e, s, pg=pg):
                    src = T[srcbuf][:, ncols * pg:ncols * (pg + 1)]
                    dst = T["xn"][:, TP * pg:TP * pg + TBH]
                    e.tensor_tensor(dst, src, T["abb"][:, 0:nstat], OP.mult)
                    e.drain()
                    e.tensor_tensor(dst, dst, T["abb"][:, TP:TP + nstat], OP.add)
                    e.drain()
                    e.tensor_scalar(dst, dst, T["nwv"][:, pg:pg + 1], None, OP.mult).then_inc(s, 1)
                vstep(app, reads=(srcbuf, "abb", "nwv"), writes=("xn",))
            elif dstkind == "x1":
                dma("sync", [(lambda: T["xr32"][:],
                              lambda pg=pg, r=r: IO["xt"][D * r + 128 * pg:D * r + 128 * (pg + 1), 3:TBH])],
                    reads=(), writes=("xr32",), sem="d_xr")

                def app(e, s, pg=pg):
                    src = T[srcbuf][:, ncols * pg:ncols * (pg + 1)]
                    dst = T["x1b"][:, ncols * pg:ncols * (pg + 1)]
                    e.tensor_tensor(dst, src, T["abb"][:, 0:nstat], OP.mult)
                    e.drain()
                    e.tensor_tensor(dst, dst, T["abb"][:, TP:TP + nstat], OP.add)
                    e.drain()
                    e.tensor_tensor(dst, dst, T["xr32"][:], OP.add).then_inc(s, 1)
                vstep(app, reads=(srcbuf, "abb", "xr32"), writes=("x1b",))
            elif dstkind == "xn2":
                def app(e, s, pg=pg):
                    src = T[srcbuf][:, ncols * pg:ncols * (pg + 1)]
                    dst = T["y2"][:, TB * pg:TB * pg + ncols]
                    e.tensor_tensor(dst, src, T["abb"][:, 0:nstat], OP.mult)
                    e.drain()
                    e.tensor_tensor(dst, dst, T["abb"][:, TP:TP + nstat], OP.add).then_inc(s, 1)
                vstep(app, reads=(srcbuf, "abb"), writes=("y2",))
            else:
                def app(e, s, pg=pg, sub=sub):
                    src = T[srcbuf][:, ncols * pg:ncols * (pg + 1)]
                    e.tensor_tensor(T["of32"][:, 0:ncols], src, T["abb"][:, 0:nstat], OP.mult)
                    e.drain()
                    e.tensor_tensor(T["of32"][:, 0:ncols], T["of32"][:, 0:ncols],
                                    T["abb"][:, TP:TP + nstat], OP.add)
                    e.drain()
                    e.tensor_tensor(T["of32"][:, 0:ncols], T["of32"][:, 0:ncols],
                                    T["x1b"][:, TB * pg + sub * TE:TB * pg + sub * TE + ncols],
                                    OP.add).then_inc(s, 1)
                vstep(app, reads=(srcbuf, "abb", "x1b"), writes=("of32",))
                dma("sync", [(lambda pg=pg, r=r, sub=sub: IO["out"][128 * pg:128 * (pg + 1), TB * r + TE * sub:TB * r + TE * (sub + 1)],
                              lambda: T["of32"][:, 0:ncols])],
                    reads=("of32",), writes=(), writes_acc=("outk",), sem="d_out")

    def phase_A(r):
        dma("gpsimd", [(lambda pg=pg: T["y2"][:, TBH * pg:TBH * (pg + 1)],
                        lambda pg=pg, r=r: IO["xt"][D * r + 128 * pg:D * r + 128 * (pg + 1), :])
                       for pg in range(DPG)],
            reads=(), writes=("y2",), sem="d_xc")
        ln_stats_apply(r, "y2", TBH, "xn", halo=True)

    def phase_B(r):
        for pg in range(EPG):
            par = pg % 2
            slot = "wsl0" if par == 0 else "wsl1"
            pp = "p1" if par == 0 else "p2"
            u0p = "u0p0" if par == 0 else "u0p1"
            cvt = "cvt0" if par == 0 else "cvt1"
            upg = "upg0" if par == 0 else "upg1"
            dma("gpsimd", [(lambda slot=slot: T[slot][:],
                            lambda pg=pg: IO["w_in4"][128 * pg:128 * (pg + 1), :])],
                reads=(), writes=(slot,), sem="d_" + slot)

            def mms(e, s, slot=slot, pp=pp):
                for kt in range(DPG):
                    e.matmul(T[pp][:, 0:TB], T[slot][:, 128 * kt:128 * (kt + 1)],
                             T["xn"][:, TP * kt:TP * kt + TB],
                             start=(kt == 0), stop=(kt == DPG - 1))
                for kt in range(DPG):
                    ins = e.matmul(T[pp][:, TB:TP], T[slot][:, 128 * kt:128 * (kt + 1)],
                                   T["xn"][:, TP * kt + TB:TP * (kt + 1)],
                                   start=(kt == 0), stop=(kt == DPG - 1))
                ins.then_inc(s, 1)
            pestep(mms, reads=(slot, "xn"), writes=(pp,))

            def ev_u0(e, s, pp=pp, u0p=u0p):
                e.activation(T[u0p][:], T[pp][:, 0:TBH], AF.Copy).then_inc(s, 1)
            sstep(ev_u0, reads=(pp,), writes=(u0p,))

            def conv(e, s, pg=pg, u0p=u0p, cvt=cvt):
                cw = T["convw"]
                e.tensor_scalar(T[cvt][:], T[u0p][:, 0:TB], cw[:, 4 * pg:4 * pg + 1], None, OP.mult)
                for k in range(1, 4):
                    e.drain()
                    e.scalar_tensor_tensor(T[cvt][:], T[u0p][:, k:TB + k],
                                           cw[:, 4 * pg + k:4 * pg + k + 1], T[cvt][:],
                                           OP.mult, OP.add)
                e.tensor_copy(T[cvt][:, 0:1], T[cvt][:, 0:1]).then_inc(s, 1)
            vstep(conv, reads=(u0p, "convw"), writes=(cvt,))

            def silu(e, s, pg=pg, cvt=cvt, upg=upg):
                e.activation(T[upg][:], T[cvt][:], AF.Silu, bias=T["cbv"][:, pg:pg + 1]).then_inc(s, 1)
            sstep(silu, reads=(cvt, "cbv"), writes=(upg,))

            dma("sync", [(lambda pg=pg, r=r: IO["g1in"][r][128 * pg:128 * (pg + 1), :],
                          lambda upg=upg: T[upg][:])],
                reads=(upg,), writes=(), writes_acc=(f"g1in{r}",), sem="d_g1w")

            def d1mm(e, s, pg=pg, upg=upg, slot=slot):
                e.matmul(T["pd"][:, 0:TB], T[slot][:, 2048:2048 + DTR], T[upg][:],
                         start=(pg == 0), stop=(pg == EPG - 1)).then_inc(s, 1)
            pestep(d1mm, reads=(upg, slot), writes=("pd",))

            def bcmm(e, s, pg=pg, upg=upg, slot=slot):
                e.matmul(T["p5"][0:32, :], T[slot][:, 2048 + DTR:2048 + DTR + 32], T[upg][:],
                         start=(pg == 0), stop=(pg == EPG - 1)).then_inc(s, 1)
            pestep(bcmm, reads=(upg, slot), writes=("p5",))

        def ev_d1(e, s):
            e.activation(T["d1c"][:], T["pd"][:, 0:TB], AF.Copy).then_inc(s, 1)
        sstep(ev_d1, reads=("pd",), writes=("d1c",))

        def ev_bc(e, s):
            e.tensor_copy(T["bct"][0:32, :], T["p5"][0:32, :]).then_inc(s, 1)
        vstep(ev_bc, reads=("p5",), writes=("bct",))

        dma("sync", [
            (lambda r=r: IO["g1in"][r][E:E + DTR, :], lambda: T["d1c"][:]),
            (lambda r=r: IO["g1in"][r][E + DTR:E + DTR + 32, :], lambda: T["bct"][0:32, :]),
        ], reads=("d1c", "bct"), writes=(), writes_acc=(f"g1in{r}",), sem="d_g1w")

        for pg in range(EPG):
            par = pg % 2
            slot = "wsl0" if par == 0 else "wsl1"
            pp = "p1" if par == 0 else "p2"
            skp = "skp0" if par == 0 else "skp1"
            dma("gpsimd", [(lambda slot=slot: T[slot][:, 0:2048],
                            lambda pg=pg: IO["w_sk4"][128 * pg:128 * (pg + 1), :])],
                reads=(), writes=(slot,), sem="d_" + slot)

            def mms(e, s, slot=slot, pp=pp):
                for kt in range(DPG):
                    ins = e.matmul(T[pp][:, 0:TB], T[slot][:, 128 * kt:128 * (kt + 1)],
                                   T["xn"][:, TP * kt + 3:TP * kt + 3 + TB],
                                   start=(kt == 0), stop=(kt == DPG - 1))
                ins.then_inc(s, 1)
            pestep(mms, reads=(slot, "xn"), writes=(pp,))

            def ev_sk(e, s, pp=pp, skp=skp):
                e.activation(T[skp][:], T[pp][:, 0:TB], AF.Copy).then_inc(s, 1)
            sstep(ev_sk, reads=(pp,), writes=(skp,))
            dma("sync", [(lambda pg=pg, r=r: IO["skip_s"][r][128 * pg:128 * (pg + 1), :],
                          lambda skp=skp: T[skp][:])],
                reads=(skp,), writes=(), writes_acc=(f"skips{r}",), sem="d_skw")

    def g1_collective(r):
        ccstep(GRP4, lambda r=r: IO["g1in"][r], lambda r=r: IO["g1out"][r],
               reads=(f"g1in{r}",), writes=(f"g1out{r}",), sem="cc_g1")

    def phase_owner(r):
        for ss in range(NSS):
            base = G1ROWS * ss
            dma("sync", [
                (lambda: T["od1"][:], lambda base=base, r=r: IO["g1out"][r][base + E:base + E + DTR, :]),
            ], reads=(f"g1out{r}",), writes=("od1",), sem="d_own")

            idx = 8 + ss

            def load_u(e, s, idx=idx, r=r):
                e.reg_load(rq0, T["ctabs"][0:1, idx:idx + 1])
                e.dma_start(out=T["oua"][:].rearrange("p (g t) -> p g t", t=TS),
                            in_=bass.AP(IO["g1out"][r].tensor, rq0,
                                        [[TB, 128], [128 * TB, QPG], [1, TB]])).then_inc(s, 16)
            plan.step("gpsimd", load_u, reads=(f"g1out{r}", "ctabs"), writes=("oua",), sem="d_ou", inc=16)

            for g in range(QPG):
                def dmm(e, s, g=g):
                    e.matmul(T["pown"][:, :], T["wd2s"][:, 128 * g:128 * (g + 1)], T["od1"][:],
                             start=True, stop=True).then_inc(s, 1)
                pestep(dmm, reads=("od1", "wd2s"), writes=("pown",))

                def softp(e, s, g=g):
                    e.activation(T["ot1"][:], T["pown"][:, :], AF.Exp, bias=T["bd2v"][:, g:g + 1])
                    e.drain()
                    e.activation(T["od"][:, TS * g:TS * (g + 1)], T["ot1"][:], AF.Ln, bias=1.0).then_inc(s, 1)
                sstep(softp, reads=("pown", "bd2v", "cpool"), writes=("od", "ot1"))

                def zmul(e, s, g=g):
                    e.tensor_tensor(T["oz"][:, TS * g:TS * (g + 1)],
                                    T["od"][:, TS * g:TS * (g + 1)],
                                    T["oua"][:, TS * g:TS * (g + 1)], OP.mult).then_inc(s, 1)
                vstep(zmul, reads=("od", "oua"), writes=("oz",))

            for n in range(NST):
                obb = "obb0" if n % 2 == 0 else "obb1"
                obr = "obr0" if n % 2 == 0 else "obr1"
                dma("sync", [(lambda obr=obr: T[obr][0:1, :].rearrange("o (b t) -> o b t", b=2),
                              lambda base=base, n=n, r=r: IO["g1out"][r][base + E + DTR + n:base + E + DTR + n + 17:NST, :].unsqueeze(0))],
                    reads=(f"g1out{r}",), writes=(obr,), sem="d_obr" + ("0" if obr == "obr0" else "1"))

                def bcast_bc(e, s, n=n, obr=obr):
                    e.matmul(T["pown"][:, :], T["ones1_16"][0:1, :],
                             T[obr][0:1, 0:TS], start=True, stop=True)
                    e.matmul(T["pd"][:, 0:TS], T["ones1_16"][0:1, :],
                             T[obr][0:1, TS:2 * TS], start=True, stop=True).then_inc(s, 1)
                pestep(bcast_bc, reads=(obr, "ones1_16"), writes=("pown", "pd"))

                def ev_bc2(e, s, obb=obb):
                    e.activation(T[obb][:, 0:TS], T["pown"][:, :], AF.Copy)
                    e.activation(T[obb][:, TS:2 * TS], T["pd"][:, 0:TS], AF.Copy).then_inc(s, 1)
                sstep(ev_bc2, reads=("pown", "pd"), writes=(obb,))

                for g in range(QPG):
                    gp = (n * QPG + g) % 2
                    oa = "oa0" if gp == 0 else "oa1"
                    ob = "ob0" if gp == 0 else "ob1"
                    oh = "oh0" if gp == 0 else "oh1"
                    otm = "otm0" if gp == 0 else "otm1"

                    def aexp(e, s, g=g, n=n, oa=oa):
                        e.activation(T[oa][:], T["od"][:, TS * g:TS * (g + 1)], AF.Exp,
                                     scale=T["a_t"][:, NST * g + n:NST * g + n + 1]).then_inc(s, 1)
                    sstep(aexp, reads=("od", "a_t"), writes=(oa,))

                    def bmul(e, s, g=g, ob=ob, obb=obb):
                        e.tensor_tensor(T[ob][:], T["oz"][:, TS * g:TS * (g + 1)],
                                        T[obb][:, 0:TS], OP.mult).then_inc(s, 1)
                    gstep(bmul, reads=("oz", obb), writes=(ob,))

                    def scan_y(e, s, g=g, n=n, oa=oa, ob=ob, oh=oh, otm=otm, obb=obb):
                        col = NST * g + n
                        e.tensor_tensor_scan(T[oh][:], T[oa][:], T[ob][:],
                                             T["hstate"][:, col:col + 1], OP.mult, OP.add)
                        e.drain()
                        e.tensor_copy(T["hstate"][:, col:col + 1], T[oh][:, TS - 1:TS])
                        e.tensor_tensor(T[otm][:], T[oh][:], T[obb][:, TS:2 * TS], OP.mult)
                        e.drain()
                        if n == 0:
                            e.tensor_copy(T["oy"][:, TS * g:TS * (g + 1)], T[otm][:]).then_inc(s, 1)
                        else:
                            e.tensor_tensor(T["oy"][:, TS * g:TS * (g + 1)],
                                            T["oy"][:, TS * g:TS * (g + 1)], T[otm][:], OP.add).then_inc(s, 1)
                    vstep(scan_y, reads=(oa, ob, obb, "oy", "hstate"),
                          writes=(oh, otm, "oy", "hstate"))

            dma("sync", [(lambda g=g, ss=ss, r=r: IO["g2in"][r][128 * g:128 * (g + 1), TS * ss:TS * (ss + 1)],
                          lambda g=g: T["oy"][:, TS * g:TS * (g + 1)])
                         for g in range(QPG)],
                reads=("oy",), writes=(), writes_acc=(f"g2in{r}",), sem="d_yw")

    def g2_collective(r):
        ccstep(GRP4, lambda r=r: IO["g2in"][r], lambda r=r: IO["g2out"][r],
               reads=(f"g2in{r}",), writes=(f"g2out{r}",), sem="cc_g2")

    def phase_CD(r):
        def load_y(e, s, r=r):
            e.reg_load(rq1, T["ctabs"][0:1, 1:2])
            e.dma_start(out=T["y2"][:].rearrange("p (a t) -> p a t", t=TB),
                        in_=bass.AP(IO["g2out"][r].tensor, rq1,
                                    [[R * TB, 128], [128 * R * TB, EPG], [1, TB]])).then_inc(s, 16)
        plan.step("gpsimd", load_y, reads=(f"g2out{r}", "ctabs"), writes=("y2",), sem="d_cy", inc=16)

        for pg in range(EPG):
            par = pg % 2
            sk2 = "sk20" if par == 0 else "sk21"
            upg = "upg0" if par == 0 else "upg1"
            dma("sync", [
                (lambda sk2=sk2: T[sk2][:], lambda pg=pg, r=r: IO["skip_s"][r][128 * pg:128 * (pg + 1), :]),
                (lambda upg=upg: T[upg][:], lambda pg=pg, r=r: IO["g1in"][r][128 * pg:128 * (pg + 1), :]),
            ], reads=(f"skips{r}", f"g1in{r}"), writes=(sk2, upg), sem="d_c" + str(pg % 2))

            def silu2(e, s, sk2=sk2):
                e.activation(T[sk2][:], T[sk2][:], AF.Silu).then_inc(s, 1)
            sstep(silu2, reads=(sk2,), writes=(sk2,))

            def y2c(e, s, pg=pg, sk2=sk2, upg=upg):
                yp = T["y2"][:, TB * pg:TB * (pg + 1)]
                e.scalar_tensor_tensor(T["cdt"][:], T[upg][:], T["wdv"][:, pg:pg + 1],
                                       yp, OP.mult, OP.add)
                e.drain()
                e.tensor_tensor(yp, T["cdt"][:], T[sk2][:], OP.mult).then_inc(s, 1)
            vstep(y2c, reads=(upg, sk2, "wdv", "y2"), writes=("y2", "cdt"))

        if IO["_debug"] and r == 0:
            dma("sync", [(lambda: IO["dbg_y2"], lambda: T["y2"][:])],
                reads=("y2",), writes=(), writes_acc=("outk",), sem="d_dbg")
        for dpg in range(DPG):
            for half in range(2):
                slot = "wsl0" if (2 * dpg + half) % 2 == 0 else "wsl1"
                dma("gpsimd", [(lambda slot=slot: T[slot][:, 0:2048],
                                lambda dpg=dpg, half=half: IO["w_out4"][128 * dpg:128 * (dpg + 1), 2048 * half:2048 * (half + 1)])],
                    reads=(), writes=(slot,), sem="d_" + slot)

                def mmo(e, s, slot=slot, half=half):
                    for kt in range(16):
                        ins = e.matmul(T["pd"][:, 0:TB], T[slot][:, 128 * kt:128 * (kt + 1)],
                                       T["y2"][:, TB * (16 * half + kt):TB * (16 * half + kt + 1)],
                                       start=(kt == 0 and half == 0), stop=(kt == 15 and half == 1))
                    ins.then_inc(s, 1)
                pestep(mmo, reads=(slot, "y2"), writes=("pd",))

            def ev_omx(e, s, dpg=dpg):
                e.activation(T["big1"][:, TB * dpg:TB * (dpg + 1)], T["pd"][:, 0:TB], AF.Copy).then_inc(s, 1)
            sstep(ev_omx, reads=("pd",), writes=("big1",))

        if IO["_debug"] and r == 0:
            dma("sync", [(lambda: IO["dbg_omx"], lambda: T["big1"][:])],
                reads=("big1",), writes=(), writes_acc=("outk",), sem="d_dbg")
        ln_stats_apply(r, "big1", TB, "x1")

    def phase_E(r):
        if IO["_debug"] and r == 0:
            dma("sync", [(lambda: IO["dbg_x1b"], lambda: T["x1b"][:])],
                reads=("x1b",), writes=(), writes_acc=("outk",), sem="d_dbg")
        ln_stats_apply(r, "x1b", TB, "xn2")
        if IO["_debug"] and r == 0:
            dma("sync", [(lambda: IO["dbg_xn2"], lambda: T["y2"][:])],
                reads=("y2",), writes=(), writes_acc=("outk",), sem="d_dbg")

        for sub in range(TB // TE):
            for hpg in range(HPG):
                par = hpg % 2
                slot = "wsl0" if par == 0 else "wsl1"
                rl = "rl0" if par == 0 else "rl1"
                rank, loc = hpg // 8, hpg % 8
                row0 = QT_ROWS * rank + 128 * (loc * 16)
                dma("sync", [(lambda slot=slot: T[slot][:, 0:2048].rearrange("p (kt c) -> p kt c", c=128),
                              lambda row0=row0: IO["wq_out"][row0:row0 + 16 * 128, :].rearrange("(kt p) c -> p kt c", p=128))],
                    reads=("wq_out",), writes=(slot,), sem="d_" + slot)

                def fcmm(e, s, slot=slot, sub=sub):
                    for kt in range(DPG):
                        ins = e.matmul(T["pe1"][:, :], T[slot][:, 128 * kt:128 * (kt + 1)],
                                       T["y2"][:, TB * kt + TE * sub:TB * kt + TE * (sub + 1)],
                                       start=(kt == 0), stop=(kt == DPG - 1))
                    ins.then_inc(s, 1)
                pestep(fcmm, reads=(slot, "y2"), writes=("pe1",))

                def relu(e, s, rl=rl):
                    e.activation(T[rl][:], T["pe1"][:, :], AF.Relu).then_inc(s, 1)
                sstep(relu, reads=("pe1",), writes=(rl,))

                def sq2(e, s, hpg=hpg, rl=rl):
                    e.tensor_tensor(T["big1"][:, TE * hpg:TE * (hpg + 1)], T[rl][:], T[rl][:], OP.mult).then_inc(s, 1)
                vstep(sq2, reads=(rl,), writes=("big1",))

            for dpg in range(DPG):
                rank, dloc = dpg // 2, dpg % 2
                for half in range(4):
                    slot = "wsl0" if half % 2 == 0 else "wsl1"
                    row0 = QT_ROWS * rank + 128 * (16 * 8 + dloc * 64 + 16 * half)
                    dma("sync", [(lambda slot=slot: T[slot][:, 0:2048].rearrange("p (kt c) -> p kt c", c=128),
                                  lambda row0=row0: IO["wq_out"][row0:row0 + 16 * 128, :].rearrange("(kt p) c -> p kt c", p=128))],
                        reads=("wq_out",), writes=(slot,), sem="d_" + slot)

                    def pjmm(e, s, slot=slot, half=half):
                        for kt in range(16):
                            ins = e.matmul(T["pe1"][:, :],
                                           T[slot][:, 128 * kt:128 * (kt + 1)],
                                           T["big1"][:, TE * (16 * half + kt):TE * (16 * half + kt + 1)],
                                           start=(kt == 0 and half == 0), stop=(kt == 15 and half == 3))
                        ins.then_inc(s, 1)
                    pestep(pjmm, reads=(slot, "big1"), writes=("pe1",))

                def ev_op(e, s, dpg=dpg):
                    e.activation(T["opre"][:, TE * dpg:TE * (dpg + 1)], T["pe1"][:, :], AF.Copy).then_inc(s, 1)
                sstep(ev_op, reads=("pe1",), writes=("opre",))

            if IO["_debug"] and r == 0 and sub == 0:
                dma("sync", [(lambda: IO["dbg_h2"], lambda: T["big1"][:]),
                             (lambda: IO["dbg_opre"], lambda: T["opre"][:])],
                    reads=("big1", "opre"), writes=(), writes_acc=("outk",), sem="d_dbg")
            ln_stats_apply(r, "opre", TE, "out", sub=sub)

    stages = []
    stages.append(("A", 0)); stages.append(("B", 0)); stages.append(("G1", 0))
    stages.append(("A", 1)); stages.append(("B", 1)); stages.append(("G1", 1))
    stages.append(("OWN", 0)); stages.append(("G2", 0))
    stages.append(("A", 2)); stages.append(("B", 2)); stages.append(("G1", 2))
    stages.append(("OWN", 1)); stages.append(("G2", 1))
    stages.append(("CD", 0)); stages.append(("E", 0))
    stages.append(("A", 3)); stages.append(("B", 3)); stages.append(("G1", 3))
    stages.append(("OWN", 2)); stages.append(("G2", 2))
    stages.append(("CD", 1)); stages.append(("E", 1))
    stages.append(("OWN", 3)); stages.append(("G2", 3))
    stages.append(("CD", 2)); stages.append(("E", 2))
    stages.append(("CD", 3)); stages.append(("E", 3))
    for kind, r in stages:
        if kind == "A":
            phase_A(r)
        elif kind == "B":
            phase_B(r)
        elif kind == "G1":
            g1_collective(r)
        elif kind == "OWN":
            phase_owner(r)
        elif kind == "G2":
            g2_collective(r)
        elif kind == "CD":
            phase_CD(r)
        elif kind == "E":
            phase_E(r)

    if IO["_debug"]:
        dma("sync", [
            (lambda: IO["dbg_g1in0"], lambda: IO["g1in"][0]),
            (lambda: IO["dbg_g1out0"], lambda: IO["g1out"][0]),
            (lambda: IO["dbg_skips0"], lambda: IO["skip_s"][0]),
            (lambda: IO["dbg_g2in0"], lambda: IO["g2in"][0]),
            (lambda: IO["dbg_g2out0"], lambda: IO["g2out"][0]),
            (lambda: IO["dbg_wqin"], lambda: IO["wq_in"]),
        ], reads=("outk", "g1in0", "g1out0", "skips0", "g2in0", "g2out0", "wq_in"),
            writes=(), writes_acc=("outk",), sem="d_dbg")

    def fin(e, s):
        pass
    plan.step("sync", fin, reads=("outk",), writes=(), inc=0)



def _tile4(wT, npg_out, npg_k):
    K, M = wT.shape
    a = wT.reshape(npg_k, 128, npg_out, 128)
    a = a.transpose(2, 1, 0, 3)
    return np.ascontiguousarray(a.reshape(npg_out * 128, npg_k * 128))


def kernel(L0_attn_x, L0_attn_norm_w, L0_attn_W_skip, L0_attn_W_in,
           L0_attn_conv_w, L0_attn_conv_b, L0_attn_Wd1, L0_attn_Wd2,
           L0_attn_bd2, L0_attn_WB, L0_attn_WC, L0_attn_A_log, L0_attn_W_D,
           L0_attn_W_out, L0_ffn_W_fc, L0_ffn_W_proj):
    x = np.asarray(L0_attn_x, np.float32)

    w_in4 = _tile4(np.ascontiguousarray(np.asarray(L0_attn_W_in, np.float32).T), EPG, DPG)
    wd1t_full = np.ascontiguousarray(np.asarray(L0_attn_Wd1, np.float32).T)
    wbct_full = np.ascontiguousarray(np.concatenate([np.asarray(L0_attn_WB, np.float32).T,
                                                     np.asarray(L0_attn_WC, np.float32).T], axis=1))
    w_in4 = np.concatenate([w_in4,
                            wd1t_full.reshape(EPG * 128, DTR),
                            wbct_full.reshape(EPG * 128, 2 * NST)], axis=1)
    w_in4 = np.ascontiguousarray(w_in4)
    w_sk4 = _tile4(np.ascontiguousarray(np.asarray(L0_attn_W_skip, np.float32).T), EPG, DPG)
    w_out4 = _tile4(np.ascontiguousarray(np.asarray(L0_attn_W_out, np.float32).T), DPG, EPG)
    wd2t = np.ascontiguousarray(np.asarray(L0_attn_Wd2, np.float32).T)
    alog = np.asarray(L0_attn_A_log, np.float32)
    convw = np.ascontiguousarray(np.asarray(L0_attn_conv_w, np.float32)[:, 0, :]
                                 .reshape(EPG, 128, 4).transpose(1, 0, 2).reshape(128, EPG * 4))
    cb = np.ascontiguousarray(np.asarray(L0_attn_conv_b, np.float32).reshape(EPG, 128).T)
    bd2_full = np.asarray(L0_attn_bd2, np.float32)
    wdv = np.ascontiguousarray(np.asarray(L0_attn_W_D, np.float32).reshape(EPG, 128).T)
    nw = np.ascontiguousarray(np.asarray(L0_attn_norm_w, np.float32).reshape(DPG, 128).T)
    wfc = np.asarray(L0_ffn_W_fc, np.float32)
    wpr = np.asarray(L0_ffn_W_proj, np.float32)

    nc = build_program()

    in_maps = []
    for c in range(8):
        s, q = c // 4, c % 4
        xt = np.zeros((R * D, TBH), np.float32)
        for r in range(R):
            p0 = 2048 * r + TB * q
            lo = max(p0 - 3, 0)
            blk = x[s, lo:p0 + TB, :].T
            xt[D * r:D * (r + 1), TBH - blk.shape[1]:] = blk
        e0 = EQ * q
        alog_own = np.ascontiguousarray(
            alog[e0:e0 + EQ].reshape(QPG, 128, NST).transpose(1, 0, 2).reshape(128, QPG * NST))
        bd2_own = np.ascontiguousarray(bd2_full[e0:e0 + EQ].reshape(QPG, 128).T)
        wd2t_own = np.ascontiguousarray(wd2t[:, e0:e0 + EQ])
        ctab = np.zeros((1, 80), np.int32)
        ctab[0, 1] = TB * q
        for ss in range(NSS):
            ctab[0, 8 + ss] = (G1ROWS * ss + EQ * q) * TB
        in_maps.append({
            "xt": xt,
            "w_in4": w_in4, "w_sk4": w_sk4, "w_out4": w_out4,
            "wd2t_own": wd2t_own,
            "alog_own": alog_own, "convw_t": convw, "cb_t": cb,
            "bd2_own": bd2_own, "wdv_t": wdv, "nw_t": nw,
            "wfc_sh": np.ascontiguousarray(wfc[1024 * c:1024 * (c + 1)]),
            "wpr_sh": np.ascontiguousarray(wpr[256 * c:256 * (c + 1)]),
            "ctab": ctab,
        })

    res = run_bass_kernel_spmd(nc, in_maps, core_ids=list(range(8)))
    import os
    if bool(int(os.environ.get("MK_DEBUG", "0"))):
        kernel._debug_results = res.results

    out = np.zeros((2, L, D), np.float32)
    for c in range(8):
        s, q = c // 4, c % 4
        ot = res.results[c]["out_t"]
        for r in range(R):
            p0 = 2048 * r + TB * q
            out[s, p0:p0 + TB, :] = ot[:, TB * r:TB * (r + 1)].T
    return out
